# version 20
# baseline (speedup 1.0000x reference)
import numpy as np
import ml_dtypes
import concourse.bass as bass
import concourse.mybir as mybir
from concourse.bass_utils import run_bass_kernel_spmd
from contextlib import ExitStack

F32 = mybir.dt.float32
BF16 = mybir.dt.bfloat16
AF = mybir.ActivationFunctionType
ALU = mybir.AluOpType
AX = mybir.AxisListType

D = 1024
DC = 8
HEADS = 16
NE = 16
NS = 2
NXP = NE + NS
LN_EPS = 1e-5
ATT_EPS = 1e-6

ENGINES = ("tensor", "vector", "scalar", "sync", "gpsimd")
QUEUES = ("qsync", "qgps", "qscal")
Q_OF_ENG = {"sync": "qsync", "gpsimd": "qgps", "scalar": "qscal"}
NSLOT = 6
QENTS = tuple(f"{q}.{s}" for q in QUEUES for s in range(NSLOT)) + ("qcc",)


class Builder:

    def __init__(self, nc):
        self.nc = nc
        self.ops = []
        self.qrot = {q: 0 for q in QUEUES}
        self.qlast = {e: None for e in QENTS}
        self.last_write = {}
        self.readers = {}
        self.acc_write = {}

    def _deps_for(self, reads, writes, acc):
        deps = {}
        def add(ent, oid):
            if deps.get(ent, -1) < oid:
                deps[ent] = oid
        for k in reads:
            lw = self.last_write.get(k)
            if lw is not None:
                add(*lw)
        for k in writes:
            for ent, oid in self.readers.get(k, {}).items():
                add(ent, oid)
            lw = self.last_write.get(k)
            if lw is not None and not (acc and self.acc_write.get(k)):
                add(*lw)
        return deps

    def op(self, eng, name, *args, reads=(), writes=(), acc=False, queue=None,
           force_inc=False, extra_dep=None, **kw):
        deps = self._deps_for(reads, writes, acc)
        if extra_dep is not None:
            prev = self.ops[extra_dep]
            if deps.get(prev["ent"], -1) < extra_dep:
                deps[prev["ent"]] = extra_dep
        oid = len(self.ops)
        ent = queue if queue is not None else eng
        rec = {"id": oid, "eng": eng, "ent": ent, "name": name, "args": args,
               "kw": kw, "deps": deps, "needed": force_inc}
        self.ops.append(rec)
        me = (ent, oid)
        for k in writes:
            self.last_write[k] = me
            self.readers[k] = {}
            self.acc_write[k] = bool(acc)
        for k in reads:
            self.readers.setdefault(k, {})[ent] = oid
        return me

    def dma(self, eng, out_ap, in_ap, reads=(), writes=(), **kw):
        q = Q_OF_ENG[eng]
        slot = self.qrot[q]
        self.qrot[q] = (slot + 1) % NSLOT
        ent = f"{q}.{slot}"
        rec = self.op(eng, "dma_start", out_ap, in_ap, reads=reads,
                      writes=writes, queue=ent, force_inc=True,
                      extra_dep=self.qlast[ent], **kw)
        self.qlast[ent] = rec[1]
        return rec

    def finalize(self):
        for rec in self.ops:
            for ent, oid in rec["deps"].items():
                self.ops[oid]["needed"] = True
        cum = {e: 0 for e in ENGINES + QENTS}
        for rec in self.ops:
            if rec["needed"]:
                cum[rec["ent"]] += 1
                rec["inc_val"] = cum[rec["ent"]]
            else:
                rec["inc_val"] = None
        self._es = ExitStack()
        semh = {e: self._es.enter_context(self.nc.semaphore(f"s_{e}"))
                for e in ENGINES + QENTS}
        self.final_counts = dict(cum)
        hwm = {e: {} for e in ENGINES}
        per_eng = {e: [] for e in ENGINES}
        for rec in self.ops:
            eng = rec["eng"]
            wl = []
            for ent, oid in sorted(rec["deps"].items()):
                val = self.ops[oid]["inc_val"]
                if ent not in ENGINES and ent != "qcc":
                    val *= 16
                if hwm[eng].get(ent, 0) >= val:
                    continue
                hwm[eng][ent] = val
                wl.append((semh[ent], val))
            inc = None
            if rec["inc_val"] is not None:
                scale = 1 if (rec["ent"] in ENGINES or rec["ent"] == "qcc") else 16
                inc = (semh[rec["ent"]], scale)
            per_eng[eng].append((wl, rec["name"], rec["args"], rec["kw"], inc))
        tail = [(semh[q], self.final_counts[q] * (1 if q == "qcc" else 16))
                for q in QENTS if self.final_counts[q] > 0]
        with self.nc.Block() as block:
            for eng in ENGINES:
                plist = per_eng[eng]
                extra = tail if eng == "sync" else []
                def body(e, plist=plist, extra=extra):
                    for wl, name, args, kw, inc in plist:
                        for sh, v in wl:
                            e.wait_ge(sh, v)
                        inst = getattr(e, name)(*args, **kw)
                        if inc is not None:
                            inst.then_inc(inc[0], inc[1])
                    for sh, v in extra:
                        e.wait_ge(sh, v)
                getattr(block, eng)(body)


def build(a_subtiles=64, sweeps=4, sw_sub=8, cc_groups=None):
    n_tok_a = a_subtiles * 128
    n_tok_out = sweeps * sw_sub * 128
    sw_tok = sw_sub * 128
    segw = min(512, sw_tok)
    nseg = sw_tok // segw
    seg_sts = [list(range(s * segw // 128, (s + 1) * segw // 128))
               for s in range(nseg)]

    nc = bass.Bass("TRN2", target_bir_lowering=False)

    x_in = nc.dram_tensor("x", [n_tok_a, D], F32, kind="ExternalInput")
    wqkv = nc.dram_tensor("wqkv", [D, 3 * D], F32, kind="ExternalInput")
    wout = nc.dram_tensor("wout", [D, D], F32, kind="ExternalInput")
    bout = nc.dram_tensor("bout", [1, D], F32, kind="ExternalInput")
    cent = nc.dram_tensor("cent", [HEADS, D], F32, kind="ExternalInput")
    ebias = nc.dram_tensor("ebias", [HEADS, 1], F32, kind="ExternalInput")
    wr_t = nc.dram_tensor("wr_t", [NE, D, D], F32, kind="ExternalInput")
    brin = nc.dram_tensor("brin", [NE, D], F32, kind="ExternalInput")
    ws_t = nc.dram_tensor("ws_t", [NS, D, D], F32, kind="ExternalInput")
    bsin = nc.dram_tensor("bsin", [NS, D], F32, kind="ExternalInput")
    y_out = nc.dram_tensor("y", [n_tok_out, D], F32, kind="ExternalOutput")

    wr_bf = nc.dram_tensor("wr_bf", [NXP, D, D], BF16, kind="Internal")
    if cc_groups is not None:
        cc_in = nc.dram_tensor("cc_in", [128, 1032], F32, kind="Internal")
        cc_out = nc.dram_tensor("cc_out", [128, 1032], F32, kind="Internal")
    wrt_dram = nc.dram_tensor("wrt", [NXP, DC, 128, D], BF16, kind="Internal")

    xn_dram = nc.dram_tensor("xn_dram", [max(n_tok_a, n_tok_out), D], BF16, kind="Internal")
    wrt_bf = nc.dram_tensor("wrt_bf", [NXP, D, D], BF16, kind="Internal")
    idn_f32 = nc.inline_tensor(np.eye(128, dtype=np.float32), name="idn_f32")
    idn_bfc = nc.inline_tensor(np.eye(128, dtype=ml_dtypes.bfloat16),
                               name="idn_bf")
    selc = np.zeros((DC, HEADS, 128), dtype=ml_dtypes.bfloat16)
    for c in range(DC):
        selc[c, 2 * c, 0:64] = 1
        selc[c, 2 * c + 1, 64:128] = 1
    sel_dram = nc.inline_tensor(selc, name="selc")
    g19c = np.zeros((NE + 3, 128), dtype=ml_dtypes.bfloat16)
    g19c[NE:NE + 2, :] = 0.5
    g19c[NE + 2, :] = 1.0
    g19c_dram = nc.inline_tensor(g19c, name="g19c")

    sb = nc.alloc_sbuf_tensor
    big_w = sb("big_w", [128, DC, 2 * D], BF16)
    wq_sb = sb("wq_sb", [128, DC, D], BF16)
    wout_sb = sb("wout_sb", [128, DC, D], BF16)
    xnf = sb("xnf", [128, DC, 2 * max(sw_tok, 256)], BF16)
    x_a = [sb(f"x_a{i}", [128, D], F32) for i in range(3)]
    xn_tm = [sb(f"xn_tm{i}", [128, D], F32) for i in range(3)]
    kp_scr = [sb(f"kp_scr{i}", [128, D], F32) for i in range(2)]
    kp_bf = [t.bitcast(BF16) for t in kp_scr]
    k_tm = [sb(f"k_tm{i}", [128, D], BF16) for i in range(2)]
    v_tm = [sb(f"v_tm{i}", [128, D], BF16) for i in range(2)]
    racc = sb("racc", [128, sw_sub, D], F32)
    af_sb = sb("af_sb", [128, DC, max(sw_tok, 1024)], BF16)
    kv_bd = sb("kv_bd", [128, DC, 128], BF16)
    ks16 = sb("ks16", [128, DC, HEADS], BF16)
    centt = sb("centt", [128, DC, HEADS], BF16)
    cent_sb = sb("cent_sb", [HEADS, D], BF16)
    sel_sb = sb("sel_sb", [HEADS, DC, 128], BF16)
    idn32 = sb("idn32", [128, 128], F32)
    idnbf = sb("idnbf", [128, 128], BF16)
    g19 = sb("g19", [NE + 3, 128], BF16)
    br19 = sb("br19", [NE + 3, D], BF16)
    eb_sb = sb("eb_sb", [HEADS, 1], F32)
    sc_sb = sb("sc_sb", [HEADS, sw_tok], F32)
    rcb = sb("rcb", [HEADS, sw_tok], BF16)
    rb_sb = sb("rb_sb", [128, 2, 512], BF16)
    g_tm = sb("g_tm", [128, sw_sub, HEADS], F32)
    ksum_acc = sb("ksum_acc", [128, DC], F32)
    cc_sb = sb("cc_sb", [128, 1032], F32) if cc_groups is not None else None
    ones_col = sb("ones_col", [128, 1], BF16)
    epsln = sb("epsln", [128, 1], F32)
    lnsc = [sb(f"lnsc{i}", [128, 8], F32) for i in range(3)]
    sd_sc = [sb(f"sd_sc{i}", [128, 1], F32) for i in range(3)]
    smx = [sb(f"smx{i}", [128, 7, HEADS], F32) for i in range(2)]
    smsc = [sb(f"smsc{i}", [128, 3], F32) for i in range(2)]

    ps = nc.alloc_psum_tensor
    pb = [ps(f"pb{i}", [128, 512], F32) for i in range(8)]
    PBK = {i: [f"pb{i}"] for i in range(6)}
    PBK[6] = ["pb6.t"]
    PBK[7] = ["pb7.t", "pb7.k"]

    B = Builder(nc)

    B.op("gpsimd", "memset", ones_col[:, :], 1.0, writes=["ones"])
    B.op("gpsimd", "memset", epsln[:, :], LN_EPS, writes=["epsln"])
    B.op("gpsimd", "memset", kv_bd.ap().rearrange("p c f -> p (c f)"), 0.0,
         writes=["kv_bd"])
    B.op("gpsimd", "memset", ks16.ap().rearrange("p c f -> p (c f)"), 0.0,
         writes=["ks16"])
    B.op("gpsimd", "memset", ksum_acc[:, :], 0.0, writes=["ksacc"])

    wqkv_r = wqkv.rearrange("(c p) f -> p c f", p=128)
    B.dma("gpsimd", big_w[:, :, :], wqkv_r[:, :, D:3 * D],
          writes=["bw0", "bw1"])
    B.dma("gpsimd", cent_sb[:, :], cent[:, :], writes=["cent"])
    B.dma("gpsimd", eb_sb[:, :], ebias[:, :], writes=["eb"])
    B.dma("gpsimd", br19[0:NE, :], brin[:, :], writes=["br19"])
    B.dma("gpsimd", br19[NE:NE + 2, :], bsin[:, :], writes=["br19x"])
    B.dma("gpsimd", br19[NE + 2:NE + 3, :], bout[:, :], writes=["br19y"])
    B.dma("gpsimd", wq_sb[:, :, :], wqkv_r[:, :, 0:D], writes=["wq"])
    wout_r = wout.rearrange("(c p) f -> p c f", p=128)
    B.dma("gpsimd", wout_sb[:, :, :], wout_r[:, :, :], writes=["wout"])

    B.dma("sync", g19c_dram[:, :], g19c_dram[:, :], writes=[]) if False else None
    B.dma("sync", g19[:, :], g19c_dram[:, :], writes=["g19", "g19c"])
    B.dma("sync", idn32[:, :], idn_f32[:, :], writes=["idn32"])
    B.dma("sync", idnbf[:, :], idn_bfc[:, :], writes=["idnbf"])
    sel_ap = bass.AP(sel_dram, 0, [[128, HEADS], [HEADS * 128, DC], [1, 128]])
    B.dma("sync", sel_sb.ap().rearrange("h c f -> h (c f)"), sel_ap,
          writes=["sel"])

    pb7_bf = pb[7].bitcast(BF16)
    for c in range(DC):
        reg = pb7_bf[:, (c % 2) * 16:(c % 2) * 16 + 16]
        B.op("tensor", "transpose", reg, cent_sb[:, c * 128:(c + 1) * 128],
             idnbf[0:HEADS, 0:HEADS], reads=["cent", "idnbf"], writes=PBK[7])
        B.op("vector", "tensor_copy", centt[:, c, :], reg, reads=PBK[7],
             writes=["centt"])

    def ln_subtile(buf, xkey):
        sc = lnsc[buf]
        sd = sd_sc[buf]
        kb = f"ln{buf}"
        xt = f"xn_tm{buf}"
        B.op("vector", "reduce_sum", sc[:, 0:1], x_a[buf][:, :], AX.X,
             reads=[xkey], writes=[kb + "s"])
        B.op("vector", "scalar_tensor_tensor", xn_tm[buf][:, :],
             x_a[buf][:, :], 1.0, x_a[buf][:, :], ALU.mult, ALU.mult,
             accum_out=sc[:, 1:2], reads=[xkey], writes=[xt, kb + "q"])
        B.op("vector", "tensor_scalar", sc[:, 2:3], sc[:, 0:1], 1.0 / D, None,
             ALU.mult, reads=[kb + "s"], writes=[kb + "m"])
        B.op("vector", "scalar_tensor_tensor", sc[:, 3:4], sc[:, 2:3], -1.0,
             sc[:, 2:3], ALU.mult, ALU.mult, reads=[kb + "m"],
             writes=[kb + "2"])
        B.op("vector", "scalar_tensor_tensor", sc[:, 4:5], sc[:, 1:2],
             1.0 / D, sc[:, 3:4], ALU.mult, ALU.add,
             reads=[kb + "q", kb + "2"], writes=[kb + "v"])
        B.op("scalar", "activation", sd[:, :], sc[:, 4:5], AF.Ln,
             bias=epsln[:, :], reads=[kb + "v", "epsln"], writes=[kb + "d"])
        B.op("scalar", "activation", sc[:, 5:6], sd[:, :], AF.Exp,
             scale=-0.5, reads=[kb + "d"], writes=[kb + "r"])
        B.op("vector", "scalar_tensor_tensor", sc[:, 6:7], sc[:, 2:3], -1.0,
             sc[:, 5:6], ALU.mult, ALU.mult, reads=[kb + "m", kb + "r"],
             writes=[kb + "n"])
        B.op("vector", "tensor_scalar", xn_tm[buf][:, :], x_a[buf][:, :],
             sc[:, 5:6], sc[:, 6:7], ALU.mult, ALU.add,
             reads=[xkey, kb + "r", kb + "n"], writes=[xt])

    def transpose_xn(buf, dest_pair_ap_fn, dest_key_fn, banks=(6, 7),
                     bkeys=("pb6.t", "pb7.t")):
        xt = f"xn_tm{buf}"
        for r in range(DC // 2):
            bk = banks[r % 2]
            rk = bkeys[r % 2]
            for j in range(2):
                c = 2 * r + j
                reg = pb[bk][:, j * 128:(j + 1) * 128]
                B.op("tensor", "transpose", reg,
                     xn_tm[buf][:, c * 128:(c + 1) * 128], idn32[:, :],
                     reads=[xt, "idn32"], writes=[rk])
            eng = "vector" if r % 2 == 0 else "scalar"
            meth = "tensor_copy" if eng == "vector" else "copy"
            B.op(eng, meth, dest_pair_ap_fn(2 * r),
                 pb[bk][:, 0:256], reads=[rk],
                 writes=[dest_key_fn(2 * r), dest_key_fn(2 * r + 1)])

    finish_kv_hook = [None]

    wrt_bufs = [big_w[:, :, 0:D], big_w[:, :, D:2 * D]]
    wrt_buf_keys = [["bw0"], ["bw1"]]

    def load_xnf_rows(base, nrows, P, xoff):
        XO = P * sw_tok + xoff
        for c in range(DC):
            in_ap = bass.AP(xn_dram, base * D + 128 * c, [[D, nrows], [1, 128]])
            B.dma("sync", xnf[:, c, XO:XO + nrows], in_ap, transpose=True,
                  reads=[f"xnd.{base // 128 + st}" for st in range(nrows // 128)],
                  writes=[f"xnf.{c}.{(xoff // 128) + st}.{P}"
                          for st in range(nrows // 128)])

    def load_xnf_at(base, P):
        XO = P * sw_tok
        for c in range(DC):
            in_ap = bass.AP(xn_dram, base * D + 128 * c, [[D, sw_tok], [1, 128]])
            B.dma("sync", xnf[:, c, XO:XO + sw_tok], in_ap, transpose=True,
                  reads=[f"xnd.{base // 128 + st}" for st in range(sw_sub)],
                  writes=[f"xnf.{c}.{st}.{P}" for st in range(sw_sub)])

    def load_xnf(sw):
        load_xnf_at(sw * sw_tok, sw % 2)

    def pre_sweep(sw):
        base = sw * sw_tok
        P = sw % 2
        XO = P * sw_tok

        if sw == 0:
            load_xnf(0)
        yield
        for c in range(DC):
            pair = (pb[0], pb[1]) if c % 2 == 0 else (pb[2], pb[3])
            pk = (PBK[0], PBK[1]) if c % 2 == 0 else (PBK[2], PBK[3])
            for dc in range(DC):
                for s in range(nseg):
                    B.op("tensor", "matmul", pair[s][:, 0:segw],
                         wq_sb[:, dc, c * 128:(c + 1) * 128],
                         xnf[:, dc, XO + s * segw:XO + (s + 1) * segw],
                         start=(dc == 0), stop=(dc == DC - 1), acc=True,
                         reads=["wq"] + [f"xnf.{dc}.{t}.{P}" for t in seg_sts[s]],
                         writes=pk[s])
            spk = f"kp{c % 2}"
            for s in range(nseg):
                seg = slice(s * segw, (s + 1) * segw)
                B.op("vector", "tensor_scalar", kp_bf[c % 2][:, seg],
                     pair[s][:, 0:segw], 0.0, None, ALU.min, reads=pk[s],
                     writes=[spk])
                B.op("vector", "tensor_scalar",
                     kp_bf[c % 2][:, 1024 + s * segw:1024 + (s + 1) * segw],
                     pair[s][:, 0:segw], 0.0, None, ALU.max, reads=pk[s],
                     writes=[spk])
                B.op("scalar", "activation", kp_bf[c % 2][:, seg],
                     kp_bf[c % 2][:, seg], AF.Exp, reads=[spk], writes=[spk])
                B.op("vector", "tensor_tensor", af_sb[:, c, seg],
                     kp_bf[c % 2][:, seg],
                     kp_bf[c % 2][:, 1024 + s * segw:1024 + (s + 1) * segw],
                     ALU.add, reads=[spk], writes=[f"af.{c}"])
            yield
        if sw == 0 and finish_kv_hook[0] is not None:
            finish_kv_hook[0]()
        for dc in range(DC):
            for s in range(nseg):
                B.op("tensor", "matmul", pb[0 + s][0:HEADS, 0:segw],
                     centt[:, dc, :],
                     xnf[:, dc, XO + s * segw:XO + (s + 1) * segw],
                     start=(dc == 0), stop=(dc == DC - 1), acc=True,
                     reads=["centt"] + [f"xnf.{dc}.{t}.{P}" for t in seg_sts[s]],
                     writes=PBK[0 + s])
                B.op("tensor", "matmul", pb[2 + s][0:HEADS, 0:segw],
                     ks16[:, dc, :], af_sb[:, dc, s * segw:(s + 1) * segw],
                     start=(dc == 0), stop=(dc == DC - 1), acc=True,
                     reads=["ks16", f"af.{dc}"], writes=PBK[2 + s])
            if dc == 3:
                yield
        yield
        for s in range(nseg):
            seg = slice(s * segw, (s + 1) * segw)
            B.op("vector", "tensor_scalar", sc_sb[:, seg],
                 pb[0 + s][0:HEADS, 0:segw], eb_sb[:, :], None, ALU.add,
                 reads=PBK[0 + s] + ["eb"], writes=[f"sc.{s}"])
        rr = kp_scr[0][0:HEADS, 0:sw_tok]
        for s in range(nseg):
            seg = slice(s * segw, (s + 1) * segw)
            B.op("vector", "tensor_scalar", rr[:, seg],
                 pb[2 + s][0:HEADS, 0:segw], ATT_EPS, None, ALU.add,
                 reads=PBK[2 + s], writes=["kp0"])
        B.op("vector", "reciprocal", rr[:, :], rr[:, :], reads=["kp0"],
             writes=["kp0"])
        B.op("vector", "tensor_copy", rcb[:, :], rr[:, :], reads=["kp0"],
             writes=["rcb"])
        yield
        for c in range(DC):
            npair = (pb[0], pb[1]) if c % 2 == 0 else (pb[2], pb[3])
            npk = (PBK[0], PBK[1]) if c % 2 == 0 else (PBK[2], PBK[3])
            rpair = (pb[2], pb[3]) if c % 2 == 0 else (pb[0], pb[1])
            rpk = (PBK[2], PBK[3]) if c % 2 == 0 else (PBK[0], PBK[1])
            for s in range(nseg):
                B.op("tensor", "matmul", npair[s][:, 0:segw], kv_bd[:, c, :],
                     af_sb[:, c, s * segw:(s + 1) * segw], start=True,
                     stop=True, reads=["kv_bd", f"af.{c}"], writes=npk[s])
                B.op("tensor", "matmul", rpair[s][:, 0:segw], sel_sb[:, c, :],
                     rcb[:, s * segw:(s + 1) * segw], start=True, stop=True,
                     reads=["sel", "rcb"], writes=rpk[s])
                B.op("scalar", "copy", rb_sb[:, s, 0:segw],
                     rpair[s][:, 0:segw], reads=rpk[s], writes=[f"rb.{s}"])
                B.op("vector", "tensor_tensor",
                     af_sb[:, c, s * segw:(s + 1) * segw],
                     npair[s][:, 0:segw], rb_sb[:, s, 0:segw], ALU.mult,
                     reads=npk[s] + [f"rb.{s}"], writes=[f"af.{c}"])
            yield
        while True:
            yield

    def softmax_st(st, P):
        buf = st % 2
        sm = smx[buf]
        ss = smsc[buf]
        smk = f"sm{buf}."
        ssk = f"ss{buf}."
        scT = pb[4][:, buf * 16:buf * 16 + 16]
        B.op("tensor", "transpose", scT,
             sc_sb[:, st * 128:(st + 1) * 128], idn32[0:HEADS, 0:HEADS],
             reads=[f"sc.{st * 128 // segw}", "idn32"], writes=PBK[4])
        B.op("vector", "reduce_max", ss[:, 0:1], scT, AX.X, negate=True,
             reads=PBK[4], writes=[ssk + "n"])
        B.op("scalar", "activation", sm[:, 0, :], scT, AF.Exp,
             bias=ss[:, 0:1], accum_out=ss[:, 1:2],
             reads=PBK[4] + [ssk + "n"], writes=[smk + "0", ssk + "s"])
        B.op("vector", "reduce_max", ss[:, 2:3], sm[:, 0, :], AX.X,
             reads=[smk + "0"], writes=[ssk + "m"])
        B.op("vector", "tensor_scalar", sm[:, 1, :], sm[:, 0, :],
             ss[:, 2:3], None, ALU.is_equal,
             reads=[smk + "0", ssk + "m"], writes=[smk + "1"])
        B.op("vector", "tensor_tensor", sm[:, 2, :], sm[:, 1, :],
             sm[:, 0, :], ALU.mult, reads=[smk + "1", smk + "0"],
             writes=[smk + "2"])
        B.op("vector", "tensor_tensor", sm[:, 3, :], sm[:, 0, :],
             sm[:, 2, :], ALU.subtract, reads=[smk + "0", smk + "2"],
             writes=[smk + "3"])
        B.op("vector", "reduce_max", ss[:, 2:3], sm[:, 3, :], AX.X,
             reads=[smk + "3"], writes=[ssk + "m"])
        B.op("vector", "tensor_scalar", sm[:, 4, :], sm[:, 3, :],
             ss[:, 2:3], None, ALU.is_equal,
             reads=[smk + "3", ssk + "m"], writes=[smk + "4"])
        B.op("vector", "tensor_tensor", sm[:, 5, :], sm[:, 1, :],
             sm[:, 4, :], ALU.add, reads=[smk + "1", smk + "4"],
             writes=[smk + "5"])
        B.op("vector", "tensor_tensor", sm[:, 6, :], sm[:, 0, :],
             sm[:, 5, :], ALU.mult, reads=[smk + "0", smk + "5"],
             writes=[smk + "6"])
        B.op("vector", "reciprocal", ss[:, 1:2], ss[:, 1:2],
             reads=[ssk + "s"], writes=[ssk + "s"])
        B.op("vector", "tensor_scalar", g_tm[:, st, :], sm[:, 6, :],
             ss[:, 1:2], None, ALU.mult, reads=[smk + "6", ssk + "s"],
             writes=[f"g.{st}"])

    def main_sweep(sw, nxt):
        base = sw * sw_tok
        P = sw % 2
        XO = P * sw_tok
        if sw + 1 < sweeps:
            load_xnf(sw + 1)
        softmax_st(0, P)
        for st in range(sw_sub):
            buf = st % 2
            gT = pb[5][0:HEADS, 0:128]
            B.op("tensor", "transpose", gT, g_tm[:, st, :], idn32[:, :],
                 reads=[f"g.{st}", "idn32"], writes=PBK[5])
            B.op("vector", "tensor_copy", g19[0:NE, :], gT, reads=PBK[5],
                 writes=["g19"])
            fpair = (pb[0], pb[1]) if buf == 0 else (pb[2], pb[3])
            fpk = (PBK[0], PBK[1]) if buf == 0 else (PBK[2], PBK[3])
            for half in range(2):
                B.op("tensor", "matmul", fpair[half][:, :], g19[:, :],
                     br19[:, half * 512:(half + 1) * 512], start=True,
                     stop=False, acc=True,
                     reads=["g19", "g19c", "br19", "br19x", "br19y"],
                     writes=fpk[half])
            for c in range(DC):
                B.op("tensor", "matmul",
                     fpair[c // 4][:, (c % 4) * 128:(c % 4) * 128 + 128],
                     xnf[:, c, XO + st * 128:XO + (st + 1) * 128], idnbf[:, :],
                     start=False, stop=False, acc=True,
                     reads=[f"xnf.{c}.{st}.{P}", "idnbf"], writes=fpk[c // 4])
            if st + 1 < sw_sub:
                softmax_st(st + 1, P)
            for c in range(DC):
                for half in range(2):
                    B.op("tensor", "matmul", fpair[half][:, :],
                         af_sb[:, c, st * 128:(st + 1) * 128],
                         wout_sb[:, c, half * 512:(half + 1) * 512],
                         start=False, stop=(c == DC - 1), acc=True,
                         reads=[f"af.{c}", "wout"], writes=fpk[half])
            xkey = f"x_a{st % 3}"
            rows = slice(base + st * 128, base + (st + 1) * 128)
            B.dma("scalar", x_a[st % 3][:, :], x_in[rows, :], writes=[xkey])
            for half in range(2):
                B.op("vector", "tensor_tensor",
                     racc[:, st, half * 512:(half + 1) * 512],
                     fpair[half][:, :],
                     x_a[st % 3][:, half * 512:(half + 1) * 512], ALU.add,
                     reads=fpk[half] + [xkey], writes=[f"racc.{st}"])

        for e in range(NXP):
            wb = e % 2
            if sw == 0:
                wsrc = wr_t if e < NE else ws_t
                eoff = (e if e < NE else e - NE) * D * D
                B.dma("gpsimd", wrt_bufs[wb],
                      bass.AP(wsrc, eoff, [[D, 128], [128 * D, DC], [1, D]]),
                      writes=wrt_buf_keys[wb])
                B.dma("sync",
                      bass.AP(wrt_bf, e * D * D,
                              [[D, 128], [128 * D, DC], [1, D]]),
                      wrt_bufs[wb],
                      reads=wrt_buf_keys[wb], writes=[f"wbf.{e}"])
            else:
                B.dma("sync", wrt_bufs[wb],
                      bass.AP(wrt_bf, e * D * D, [[D, 128], [128 * D, DC], [1, D]]),
                      reads=[f"wbf.{e}"], writes=wrt_buf_keys[wb])
            for st in range(sw_sub):
                par = (e * sw_sub + st) % 2
                xpair = (pb[4], pb[5]) if par == 0 else (pb[6], pb[7])
                xpk = (PBK[4], PBK[5]) if par == 0 else (PBK[6], PBK[7])
                for c in range(DC):
                    for half in range(2):
                        B.op("tensor", "matmul", xpair[half][:, :],
                             xnf[:, c, XO + st * 128:XO + (st + 1) * 128],
                             big_w[:, c, wb * D + half * 512:wb * D + (half + 1) * 512],
                             start=(c == 0), stop=(c == DC - 1), acc=True,
                             reads=[f"xnf.{c}.{st}.{P}"] + wrt_buf_keys[wb],
                             writes=xpk[half])
                gsc = g_tm[:, st, e:e + 1] if e < NE else 0.5
                grd = [f"g.{st}"] if e < NE else []
                for half in range(2):
                    B.op("vector", "scalar_tensor_tensor",
                         racc[:, st, half * 512:(half + 1) * 512],
                         xpair[half][:, :], gsc,
                         racc[:, st, half * 512:(half + 1) * 512],
                         ALU.mult, ALU.add,
                         reads=xpk[half] + grd + [f"racc.{st}"],
                         writes=[f"racc.{st}"])
            next(nxt)
            if e % 2 == 1:
                next(nxt)
        for _ in range(40):
            next(nxt)
        for st in range(sw_sub):
            rows = slice(base + st * 128, base + (st + 1) * 128)
            B.dma("scalar", y_out[rows, :], racc[:, st, :],
                  reads=[f"racc.{st}"])

    def empty_gen():
        while True:
            yield


    gens = [pre_sweep(sw) for sw in range(sweeps)]
    pa0_left = [2 if sweeps > 0 else 0]

    def a_xload(st):
        buf = st % 3
        B.dma("scalar", x_a[buf][:, :], x_in[st * 128:(st + 1) * 128, :],
              writes=[f"x_a{buf}"])

    assert a_subtiles % sw_sub == 0
    n_asw = a_subtiles // sw_sub

    ramp = min(sw_sub, a_subtiles)
    ln_seq = list(range(ramp))
    for _asw in range(a_subtiles // sw_sub):
        for _j in range(sw_sub):
            _st = _asw * sw_sub + _j
            if _asw == 0 and ramp < sw_sub and ramp + _j < sw_sub:
                ln_seq.append(ramp + _j)
            if _st + sw_sub < a_subtiles:
                ln_seq.append(_st + sw_sub)
    assert sorted(ln_seq) == list(range(a_subtiles))
    ln_pos = [0]

    def a_xload_buf(st, buf):
        B.dma("scalar", x_a[buf][:, :], x_in[st * 128:(st + 1) * 128, :],
              writes=[f"x_a{buf}"])

    def ln_store(s):
        pos = ln_pos[0]
        assert ln_seq[pos] == s, (pos, s, ln_seq[pos])
        if pos == 0:
            for ahead in range(min(2, len(ln_seq))):
                a_xload_buf(ln_seq[ahead], ahead % 3)
        if pos + 2 < len(ln_seq):
            a_xload_buf(ln_seq[pos + 2], (pos + 2) % 3)
        buf = pos % 3
        ln_subtile(buf, f"x_a{buf}")
        B.dma("gpsimd", xn_dram[s * 128:(s + 1) * 128, :],
              xn_tm[buf][:, :], reads=[f"xn_tm{buf}"], writes=[f"xnd.{s}"])
        ln_pos[0] += 1

    for st in range(ramp):
        ln_store(st)

    half_tok = ramp * 128
    for asw in range(n_asw):
        PA = (asw + 1) % 2
        XOA = PA * sw_tok
        if asw == 0 and ramp < sw_sub:
            load_xnf_rows(0, half_tok, PA, 0)
        else:
            load_xnf_at(asw * sw_tok, PA)
        for j in range(sw_sub):
            st = asw * sw_sub + j
            if asw == 0 and ramp < sw_sub and j == ramp:
                load_xnf_rows(half_tok, sw_tok - half_tok, PA, half_tok)
            buf2 = st % 2
            kpair = (pb[0], pb[1])
            vpair = (pb[2], pb[3])
            for c in range(DC):
                lhsT = xnf[:, c, XOA + j * 128:XOA + (j + 1) * 128]
                xk = f"xnf.{c}.{j}.{PA}"
                for half in range(2):
                    B.op("tensor", "matmul", kpair[half][:, :], lhsT,
                         big_w[:, c, half * 512:(half + 1) * 512],
                         start=(c == 0), stop=(c == DC - 1), acc=True,
                         reads=[xk, "bw0"], writes=PBK[half])
            ktk, vtk, spk = f"k_tm{buf2}", f"v_tm{buf2}", f"kp{buf2}"
            for half in range(2):
                seg = slice(half * 512, (half + 1) * 512)
                B.op("vector", "tensor_scalar", kp_bf[buf2][:, seg],
                     kpair[half][:, :], 0.0, None, ALU.min, reads=PBK[half],
                     writes=[spk])
                B.op("vector", "tensor_scalar",
                     kp_bf[buf2][:, 1024 + half * 512:1024 + (half + 1) * 512],
                     kpair[half][:, :], 0.0, None, ALU.max, reads=PBK[half],
                     writes=[spk])
            for c in range(DC):
                lhsT = xnf[:, c, XOA + j * 128:XOA + (j + 1) * 128]
                xk = f"xnf.{c}.{j}.{PA}"
                for half in range(2):
                    B.op("tensor", "matmul", vpair[half][:, :], lhsT,
                         big_w[:, c, D + half * 512:D + (half + 1) * 512],
                         start=(c == 0), stop=(c == DC - 1), acc=True,
                         reads=[xk, "bw1"], writes=PBK[2 + half])
            for half in range(2):
                seg = slice(half * 512, (half + 1) * 512)
                B.op("scalar", "activation", kp_bf[buf2][:, seg],
                     kp_bf[buf2][:, seg], AF.Exp, reads=[spk], writes=[spk])
                B.op("vector", "tensor_tensor", k_tm[buf2][:, seg],
                     kp_bf[buf2][:, seg],
                     kp_bf[buf2][:, 1024 + half * 512:1024 + (half + 1) * 512],
                     ALU.add, reads=[spk], writes=[ktk])
                B.op("scalar", "copy", v_tm[buf2][:, seg], vpair[half][:, :],
                     reads=PBK[2 + half], writes=[vtk])
            for p in range(DC):
                bk = 4 + p // 4
                B.op("tensor", "matmul",
                     pb[bk][:, 128 * (p % 4):128 * (p % 4) + 128],
                     k_tm[buf2][:, 128 * p:128 * p + 128],
                     v_tm[buf2][:, 128 * p:128 * p + 128],
                     start=(st == 0 and p % 4 == 0),
                     stop=(st == a_subtiles - 1 and p % 4 == 3),
                     acc=True, reads=[ktk, vtk], writes=PBK[bk])
                B.op("tensor", "matmul", pb[7][:, 256 + p:256 + p + 1],
                     k_tm[buf2][:, 128 * p:128 * p + 128], ones_col[:, :],
                     start=(p == 0), stop=(p == DC - 1), acc=True,
                     reads=[ktk, "ones"], writes=["pb7.k"])
            if asw == 0 and ramp < sw_sub and ramp + j < sw_sub:
                ln_store(ramp + j)
            if st + sw_sub < a_subtiles:
                ln_store(st + sw_sub)
            B.op("vector", "tensor_tensor", ksum_acc[:, :], ksum_acc[:, :],
                 pb[7][:, 256:256 + DC], ALU.add, reads=["pb7.k", "ksacc"],
                 writes=["ksacc"])
            if (sweeps > 0 and st >= a_subtiles - sw_sub + 1
                    and pa0_left[0] > 0):
                next(gens[0])
                pa0_left[0] -= 1

    if cc_groups is not None:
        B.op("vector", "tensor_copy", cc_sb[:, 0:512], pb[4][:, :],
             reads=PBK[4], writes=["ccs"])
        B.op("vector", "tensor_copy", cc_sb[:, 512:1024], pb[5][:, :],
             reads=PBK[5], writes=["ccs"])
        B.op("vector", "tensor_copy", cc_sb[:, 1024:1032], ksum_acc[:, :],
             reads=["ksacc"], writes=["ccs"])
        B.dma("gpsimd", cc_in[:, :], cc_sb[:, :], reads=["ccs"],
              writes=["ccin"])
        B.op("gpsimd", "collective_compute", "AllReduce", ALU.add,
             replica_groups=cc_groups, ins=[cc_in.ap().opt()],
             outs=[cc_out.ap().opt()], reads=["ccin"], writes=["ccout"],
             queue="qcc", force_inc=True)
        B.dma("gpsimd", cc_sb[:, :], cc_out[:, :], reads=["ccout"],
              writes=["ccs"])

        def finish_kv():
            for p in range(DC):
                blk = cc_sb[:, 128 * p:128 * p + 128]
                B.op("vector", "tensor_copy", kv_bd[0:64, p, 0:64],
                     blk[0:64, 0:64], reads=["ccs"], writes=["kv_bd"])
                B.op("vector", "tensor_copy", kv_bd[64:128, p, 64:128],
                     blk[64:128, 64:128], reads=["ccs"], writes=["kv_bd"])
                B.op("vector", "tensor_copy", ks16[0:64, p, 2 * p:2 * p + 1],
                     cc_sb[0:64, 1024 + p:1025 + p], reads=["ccs"],
                     writes=["ks16"])
                B.op("vector", "tensor_copy",
                     ks16[64:128, p, 2 * p + 1:2 * p + 2],
                     cc_sb[64:128, 1024 + p:1025 + p], reads=["ccs"],
                     writes=["ks16"])
        finish_kv_hook[0] = finish_kv
    else:
        for p in range(DC):
            bk = 4 + p // 4
            blk = pb[bk][:, 128 * (p % 4):128 * (p % 4) + 128]
            B.op("vector", "tensor_copy", kv_bd[0:64, p, 0:64],
                 blk[0:64, 0:64], reads=PBK[bk], writes=["kv_bd"])
            B.op("vector", "tensor_copy", kv_bd[64:128, p, 64:128],
                 blk[64:128, 64:128], reads=PBK[bk], writes=["kv_bd"])
            B.op("vector", "tensor_copy", ks16[0:64, p, 2 * p:2 * p + 1],
                 ksum_acc[0:64, p:p + 1], reads=["ksacc"], writes=["ks16"])
            B.op("vector", "tensor_copy", ks16[64:128, p, 2 * p + 1:2 * p + 2],
                 ksum_acc[64:128, p:p + 1], reads=["ksacc"], writes=["ks16"])

    for _ in range(100):
        next(gens[0])
    for sw in range(sweeps):
        nxt = gens[sw + 1] if sw + 1 < sweeps else empty_gen()
        main_sweep(sw, nxt)

    B.finalize()
    return nc


_NC_CACHE = {}


CC_GROUPS = [[0, 1], [2, 3], [4, 5], [6, 7]]


def _get_nc():
    if "full" not in _NC_CACHE:
        _NC_CACHE["full"] = build(a_subtiles=32, cc_groups=CC_GROUPS)
    return _NC_CACHE["full"]


def kernel(L0_attn_x, L0_attn_Wqkv, L0_attn_Wout, L0_attn_bout,
           L0_ffn_centroids, L0_ffn_expert_biases, L0_ffn_Wr, L0_ffn_br,
           L0_ffn_Ws, L0_ffn_bs):
    x = np.ascontiguousarray(np.asarray(L0_attn_x, dtype=np.float32))
    Bsz, N, _ = x.shape
    half = N // 2
    nc = _get_nc()
    common = {
        "wqkv": np.ascontiguousarray(L0_attn_Wqkv, dtype=np.float32),
        "wout": np.ascontiguousarray(L0_attn_Wout, dtype=np.float32),
        "bout": np.ascontiguousarray(L0_attn_bout, dtype=np.float32).reshape(1, D),
        "cent": np.ascontiguousarray(L0_ffn_centroids, dtype=np.float32),
        "ebias": np.ascontiguousarray(L0_ffn_expert_biases,
                                      dtype=np.float32).reshape(HEADS, 1),
        "wr_t": np.ascontiguousarray(
            np.asarray(L0_ffn_Wr, dtype=np.float32).transpose(0, 2, 1)),
        "brin": np.ascontiguousarray(L0_ffn_br, dtype=np.float32),
        "ws_t": np.ascontiguousarray(
            np.asarray(L0_ffn_Ws, dtype=np.float32).transpose(0, 2, 1)),
        "bsin": np.ascontiguousarray(L0_ffn_bs, dtype=np.float32),
    }
    in_maps = []
    for i in range(8):
        b, h = i // 2, i % 2
        m = dict(common)
        m["x"] = np.ascontiguousarray(x[b, h * half:(h + 1) * half, :])
        in_maps.append(m)
    res = run_bass_kernel_spmd(nc, in_maps, core_ids=list(range(8)))
    y = np.empty((Bsz, N, D), dtype=np.float32)
    for i in range(8):
        b, h = i // 2, i % 2
        y[b, h * half:(h + 1) * half, :] = res.results[i]["y"]
    return y


# version 22
# speedup vs baseline: 1.0488x; 1.0488x over previous
import numpy as np
import ml_dtypes
import concourse.bass as bass
import concourse.mybir as mybir
from concourse.bass_utils import run_bass_kernel_spmd
from contextlib import ExitStack

F32 = mybir.dt.float32
BF16 = mybir.dt.bfloat16
AF = mybir.ActivationFunctionType
ALU = mybir.AluOpType
AX = mybir.AxisListType

D = 1024
DC = 8
HEADS = 16
NE = 16
NS = 2
NXP = NE + NS
LN_EPS = 1e-5
ATT_EPS = 1e-6

ENGINES = ("tensor", "vector", "scalar", "sync", "gpsimd")
QUEUES = ("qsync", "qgps", "qscal")
Q_OF_ENG = {"sync": "qsync", "gpsimd": "qgps", "scalar": "qscal"}
NSLOT = 6
QENTS = tuple(f"{q}.{s}" for q in QUEUES for s in range(NSLOT)) + ("qcc",)


class Builder:

    def __init__(self, nc):
        self.nc = nc
        self.ops = []
        self.qrot = {q: 0 for q in QUEUES}
        self.qlast = {e: None for e in QENTS}
        self.last_write = {}
        self.readers = {}
        self.acc_write = {}

    def _deps_for(self, reads, writes, acc):
        deps = {}
        def add(ent, oid):
            if deps.get(ent, -1) < oid:
                deps[ent] = oid
        for k in reads:
            lw = self.last_write.get(k)
            if lw is not None:
                add(*lw)
        for k in writes:
            for ent, oid in self.readers.get(k, {}).items():
                add(ent, oid)
            lw = self.last_write.get(k)
            if lw is not None and not (acc and self.acc_write.get(k)):
                add(*lw)
        return deps

    def op(self, eng, name, *args, reads=(), writes=(), acc=False, queue=None,
           force_inc=False, extra_dep=None, **kw):
        deps = self._deps_for(reads, writes, acc)
        if extra_dep is not None:
            prev = self.ops[extra_dep]
            if deps.get(prev["ent"], -1) < extra_dep:
                deps[prev["ent"]] = extra_dep
        oid = len(self.ops)
        ent = queue if queue is not None else eng
        rec = {"id": oid, "eng": eng, "ent": ent, "name": name, "args": args,
               "kw": kw, "deps": deps, "needed": force_inc}
        self.ops.append(rec)
        me = (ent, oid)
        for k in writes:
            self.last_write[k] = me
            self.readers[k] = {}
            self.acc_write[k] = bool(acc)
        for k in reads:
            self.readers.setdefault(k, {})[ent] = oid
        return me

    def dma(self, eng, out_ap, in_ap, reads=(), writes=(), **kw):
        q = Q_OF_ENG[eng]
        slot = self.qrot[q]
        self.qrot[q] = (slot + 1) % NSLOT
        ent = f"{q}.{slot}"
        rec = self.op(eng, "dma_start", out_ap, in_ap, reads=reads,
                      writes=writes, queue=ent, force_inc=True,
                      extra_dep=self.qlast[ent], **kw)
        self.qlast[ent] = rec[1]
        return rec

    def finalize(self):
        for rec in self.ops:
            for ent, oid in rec["deps"].items():
                self.ops[oid]["needed"] = True
        cum = {e: 0 for e in ENGINES + QENTS}
        for rec in self.ops:
            if rec["needed"]:
                cum[rec["ent"]] += 1
                rec["inc_val"] = cum[rec["ent"]]
            else:
                rec["inc_val"] = None
        self._es = ExitStack()
        semh = {e: self._es.enter_context(self.nc.semaphore(f"s_{e}"))
                for e in ENGINES + QENTS}
        self.final_counts = dict(cum)
        hwm = {e: {} for e in ENGINES}
        per_eng = {e: [] for e in ENGINES}
        for rec in self.ops:
            eng = rec["eng"]
            wl = []
            for ent, oid in sorted(rec["deps"].items()):
                val = self.ops[oid]["inc_val"]
                if ent not in ENGINES and ent != "qcc":
                    val *= 16
                if hwm[eng].get(ent, 0) >= val:
                    continue
                hwm[eng][ent] = val
                wl.append((semh[ent], val))
            inc = None
            if rec["inc_val"] is not None:
                scale = 1 if (rec["ent"] in ENGINES or rec["ent"] == "qcc") else 16
                inc = (semh[rec["ent"]], scale)
            per_eng[eng].append((wl, rec["name"], rec["args"], rec["kw"], inc))
        tail = [(semh[q], self.final_counts[q] * (1 if q == "qcc" else 16))
                for q in QENTS if self.final_counts[q] > 0]
        with self.nc.Block() as block:
            for eng in ENGINES:
                plist = per_eng[eng]
                extra = tail if eng == "sync" else []
                def body(e, plist=plist, extra=extra):
                    for wl, name, args, kw, inc in plist:
                        for sh, v in wl:
                            e.wait_ge(sh, v)
                        inst = getattr(e, name)(*args, **kw)
                        if inc is not None:
                            inst.then_inc(inc[0], inc[1])
                    for sh, v in extra:
                        e.wait_ge(sh, v)
                getattr(block, eng)(body)


def build(a_subtiles=64, sweeps=4, sw_sub=8, cc_groups=None):
    n_tok_a = a_subtiles * 128
    n_tok_out = sweeps * sw_sub * 128
    sw_tok = sw_sub * 128
    segw = min(512, sw_tok)
    nseg = sw_tok // segw
    seg_sts = [list(range(s * segw // 128, (s + 1) * segw // 128))
               for s in range(nseg)]

    nc = bass.Bass("TRN2", target_bir_lowering=False)

    x_in = nc.dram_tensor("x", [n_tok_a, D], F32, kind="ExternalInput")
    wqkv = nc.dram_tensor("wqkv", [D, 3 * D], F32, kind="ExternalInput")
    wout = nc.dram_tensor("wout", [D, D], F32, kind="ExternalInput")
    bout = nc.dram_tensor("bout", [1, D], F32, kind="ExternalInput")
    cent = nc.dram_tensor("cent", [HEADS, D], F32, kind="ExternalInput")
    ebias = nc.dram_tensor("ebias", [HEADS, 1], F32, kind="ExternalInput")
    wr_t = nc.dram_tensor("wr_t", [NE, D, D], F32, kind="ExternalInput")
    brin = nc.dram_tensor("brin", [NE, D], F32, kind="ExternalInput")
    ws_t = nc.dram_tensor("ws_t", [NS, D, D], F32, kind="ExternalInput")
    bsin = nc.dram_tensor("bsin", [NS, D], F32, kind="ExternalInput")
    y_out = nc.dram_tensor("y", [n_tok_out, D], F32, kind="ExternalOutput")

    wr_bf = nc.dram_tensor("wr_bf", [NXP, D, D], BF16, kind="Internal")
    if cc_groups is not None:
        cc_in = nc.dram_tensor("cc_in", [128, 1032], F32, kind="Internal")
        cc_out = nc.dram_tensor("cc_out", [128, 1032], F32, kind="Internal")
    wrt_dram = nc.dram_tensor("wrt", [NXP, DC, 128, D], BF16, kind="Internal")

    xn_dram = nc.dram_tensor("xn_dram", [max(n_tok_a, n_tok_out), D], BF16, kind="Internal")
    idn_f32 = nc.inline_tensor(np.eye(128, dtype=np.float32), name="idn_f32")
    idn_bfc = nc.inline_tensor(np.eye(128, dtype=ml_dtypes.bfloat16),
                               name="idn_bf")
    selc = np.zeros((DC, HEADS, 128), dtype=ml_dtypes.bfloat16)
    for c in range(DC):
        selc[c, 2 * c, 0:64] = 1
        selc[c, 2 * c + 1, 64:128] = 1
    sel_dram = nc.inline_tensor(selc, name="selc")
    g19c = np.zeros((NE + 3, 128), dtype=ml_dtypes.bfloat16)
    g19c[NE:NE + 2, :] = 0.5
    g19c[NE + 2, :] = 1.0
    g19c_dram = nc.inline_tensor(g19c, name="g19c")

    sb = nc.alloc_sbuf_tensor
    big_w = sb("big_w", [128, DC, 2 * D], BF16)
    wq_sb = sb("wq_sb", [128, DC, D], BF16)
    wout_sb = sb("wout_sb", [128, DC, D], BF16)
    xnf = sb("xnf", [128, DC, 2 * max(sw_tok, 256)], BF16)
    x_a = [sb(f"x_a{i}", [128, D], F32) for i in range(3)]
    xn_tm = [sb(f"xn_tm{i}", [128, D], F32) for i in range(3)]
    kp_scr = [sb(f"kp_scr{i}", [128, D], F32) for i in range(2)]
    kp_bf = [t.bitcast(BF16) for t in kp_scr]
    k_tm = [sb(f"k_tm{i}", [128, D], BF16) for i in range(2)]
    v_tm = [sb(f"v_tm{i}", [128, D], BF16) for i in range(2)]
    racc = sb("racc", [128, sw_sub, D], F32)
    af_sb = sb("af_sb", [128, DC, max(sw_tok, 1024)], BF16)
    kv_bd = sb("kv_bd", [128, DC, 128], BF16)
    ks16 = sb("ks16", [128, DC, HEADS], BF16)
    centt = sb("centt", [128, DC, HEADS], BF16)
    cent_sb = sb("cent_sb", [HEADS, D], BF16)
    sel_sb = sb("sel_sb", [HEADS, DC, 128], BF16)
    idn32 = sb("idn32", [128, 128], F32)
    idnbf = sb("idnbf", [128, 128], BF16)
    g19 = sb("g19", [NE + 3, 128], BF16)
    br19 = sb("br19", [NE + 3, D], BF16)
    eb_sb = sb("eb_sb", [HEADS, 1], F32)
    sc_sb = sb("sc_sb", [HEADS, sw_tok], F32)
    rcb = sb("rcb", [HEADS, sw_tok], BF16)
    rb_sb = sb("rb_sb", [128, 2, 512], BF16)
    g_tm = sb("g_tm", [128, sw_sub, HEADS], F32)
    ksum_acc = sb("ksum_acc", [128, DC], F32)
    cc_sb = sb("cc_sb", [128, 1032], F32) if cc_groups is not None else None
    ones_col = sb("ones_col", [128, 1], BF16)
    epsln = sb("epsln", [128, 1], F32)
    lnsc = [sb(f"lnsc{i}", [128, 8], F32) for i in range(3)]
    sd_sc = [sb(f"sd_sc{i}", [128, 1], F32) for i in range(3)]
    smx = [sb(f"smx{i}", [128, 7, HEADS], F32) for i in range(2)]
    smsc = [sb(f"smsc{i}", [128, 3], F32) for i in range(2)]

    ps = nc.alloc_psum_tensor
    pb = [ps(f"pb{i}", [128, 512], F32) for i in range(8)]
    PBK = {i: [f"pb{i}"] for i in range(6)}
    PBK[6] = ["pb6.t"]
    PBK[7] = ["pb7.t", "pb7.k"]

    B = Builder(nc)

    B.op("gpsimd", "memset", ones_col[:, :], 1.0, writes=["ones"])
    B.op("gpsimd", "memset", epsln[:, :], LN_EPS, writes=["epsln"])
    B.op("gpsimd", "memset", kv_bd.ap().rearrange("p c f -> p (c f)"), 0.0,
         writes=["kv_bd"])
    B.op("gpsimd", "memset", ks16.ap().rearrange("p c f -> p (c f)"), 0.0,
         writes=["ks16"])
    B.op("gpsimd", "memset", ksum_acc[:, :], 0.0, writes=["ksacc"])

    wqkv_r = wqkv.rearrange("(c p) f -> p c f", p=128)
    B.dma("gpsimd", big_w[:, :, :], wqkv_r[:, :, D:3 * D],
          writes=["bw0", "bw1"])
    B.dma("gpsimd", cent_sb[:, :], cent[:, :], writes=["cent"])
    B.dma("gpsimd", eb_sb[:, :], ebias[:, :], writes=["eb"])
    B.dma("gpsimd", br19[0:NE, :], brin[:, :], writes=["br19"])
    B.dma("gpsimd", br19[NE:NE + 2, :], bsin[:, :], writes=["br19x"])
    B.dma("gpsimd", br19[NE + 2:NE + 3, :], bout[:, :], writes=["br19y"])
    B.dma("gpsimd", wq_sb[:, :, :], wqkv_r[:, :, 0:D], writes=["wq"])
    wout_r = wout.rearrange("(c p) f -> p c f", p=128)
    B.dma("gpsimd", wout_sb[:, :, :], wout_r[:, :, :], writes=["wout"])

    B.dma("sync", g19c_dram[:, :], g19c_dram[:, :], writes=[]) if False else None
    B.dma("sync", g19[:, :], g19c_dram[:, :], writes=["g19", "g19c"])
    B.dma("sync", idn32[:, :], idn_f32[:, :], writes=["idn32"])
    B.dma("sync", idnbf[:, :], idn_bfc[:, :], writes=["idnbf"])
    sel_ap = bass.AP(sel_dram, 0, [[128, HEADS], [HEADS * 128, DC], [1, 128]])
    B.dma("sync", sel_sb.ap().rearrange("h c f -> h (c f)"), sel_ap,
          writes=["sel"])

    pb7_bf = pb[7].bitcast(BF16)
    for c in range(DC):
        reg = pb7_bf[:, (c % 2) * 16:(c % 2) * 16 + 16]
        B.op("tensor", "transpose", reg, cent_sb[:, c * 128:(c + 1) * 128],
             idnbf[0:HEADS, 0:HEADS], reads=["cent", "idnbf"], writes=PBK[7])
        B.op("vector", "tensor_copy", centt[:, c, :], reg, reads=PBK[7],
             writes=["centt"])

    def ln_subtile(buf, xkey):
        sc = lnsc[buf]
        sd = sd_sc[buf]
        kb = f"ln{buf}"
        xt = f"xn_tm{buf}"
        B.op("vector", "reduce_sum", sc[:, 0:1], x_a[buf][:, :], AX.X,
             reads=[xkey], writes=[kb + "s"])
        B.op("vector", "scalar_tensor_tensor", xn_tm[buf][:, :],
             x_a[buf][:, :], 1.0, x_a[buf][:, :], ALU.mult, ALU.mult,
             accum_out=sc[:, 1:2], reads=[xkey], writes=[xt, kb + "q"])
        B.op("vector", "tensor_scalar", sc[:, 2:3], sc[:, 0:1], 1.0 / D, None,
             ALU.mult, reads=[kb + "s"], writes=[kb + "m"])
        B.op("vector", "scalar_tensor_tensor", sc[:, 3:4], sc[:, 2:3], -1.0,
             sc[:, 2:3], ALU.mult, ALU.mult, reads=[kb + "m"],
             writes=[kb + "2"])
        B.op("vector", "scalar_tensor_tensor", sc[:, 4:5], sc[:, 1:2],
             1.0 / D, sc[:, 3:4], ALU.mult, ALU.add,
             reads=[kb + "q", kb + "2"], writes=[kb + "v"])
        B.op("scalar", "activation", sd[:, :], sc[:, 4:5], AF.Ln,
             bias=epsln[:, :], reads=[kb + "v", "epsln"], writes=[kb + "d"])
        B.op("scalar", "activation", sc[:, 5:6], sd[:, :], AF.Exp,
             scale=-0.5, reads=[kb + "d"], writes=[kb + "r"])
        B.op("vector", "scalar_tensor_tensor", sc[:, 6:7], sc[:, 2:3], -1.0,
             sc[:, 5:6], ALU.mult, ALU.mult, reads=[kb + "m", kb + "r"],
             writes=[kb + "n"])
        B.op("vector", "tensor_scalar", xn_tm[buf][:, :], x_a[buf][:, :],
             sc[:, 5:6], sc[:, 6:7], ALU.mult, ALU.add,
             reads=[xkey, kb + "r", kb + "n"], writes=[xt])

    def transpose_xn(buf, dest_pair_ap_fn, dest_key_fn, banks=(6, 7),
                     bkeys=("pb6.t", "pb7.t")):
        xt = f"xn_tm{buf}"
        for r in range(DC // 2):
            bk = banks[r % 2]
            rk = bkeys[r % 2]
            for j in range(2):
                c = 2 * r + j
                reg = pb[bk][:, j * 128:(j + 1) * 128]
                B.op("tensor", "transpose", reg,
                     xn_tm[buf][:, c * 128:(c + 1) * 128], idn32[:, :],
                     reads=[xt, "idn32"], writes=[rk])
            eng = "vector" if r % 2 == 0 else "scalar"
            meth = "tensor_copy" if eng == "vector" else "copy"
            B.op(eng, meth, dest_pair_ap_fn(2 * r),
                 pb[bk][:, 0:256], reads=[rk],
                 writes=[dest_key_fn(2 * r), dest_key_fn(2 * r + 1)])

    finish_kv_hook = [None]

    wrt_bufs = [big_w[:, :, 0:D], big_w[:, :, D:2 * D]]
    wrt_buf_keys = [["bw0"], ["bw1"]]

    def load_xnf_rows(base, nrows, P, xoff):
        XO = P * sw_tok + xoff
        for c in range(DC):
            in_ap = bass.AP(xn_dram, base * D + 128 * c, [[D, nrows], [1, 128]])
            B.dma("sync", xnf[:, c, XO:XO + nrows], in_ap, transpose=True,
                  reads=[f"xnd.{base // 128 + st}" for st in range(nrows // 128)],
                  writes=[f"xnf.{c}.{(xoff // 128) + st}.{P}"
                          for st in range(nrows // 128)])

    def load_xnf_at(base, P):
        XO = P * sw_tok
        for c in range(DC):
            in_ap = bass.AP(xn_dram, base * D + 128 * c, [[D, sw_tok], [1, 128]])
            B.dma("sync", xnf[:, c, XO:XO + sw_tok], in_ap, transpose=True,
                  reads=[f"xnd.{base // 128 + st}" for st in range(sw_sub)],
                  writes=[f"xnf.{c}.{st}.{P}" for st in range(sw_sub)])

    def load_xnf(sw):
        load_xnf_at(sw * sw_tok, sw % 2)

    def pre_sweep(sw):
        base = sw * sw_tok
        P = sw % 2
        XO = P * sw_tok

        if sw == 0:
            load_xnf(0)
        yield
        for c in range(DC):
            pair = (pb[0], pb[1]) if c % 2 == 0 else (pb[2], pb[3])
            pk = (PBK[0], PBK[1]) if c % 2 == 0 else (PBK[2], PBK[3])
            for dc in range(DC):
                for s in range(nseg):
                    B.op("tensor", "matmul", pair[s][:, 0:segw],
                         wq_sb[:, dc, c * 128:(c + 1) * 128],
                         xnf[:, dc, XO + s * segw:XO + (s + 1) * segw],
                         start=(dc == 0), stop=(dc == DC - 1), acc=True,
                         reads=["wq"] + [f"xnf.{dc}.{t}.{P}" for t in seg_sts[s]],
                         writes=pk[s])
            spk = f"kp{c % 2}"
            for s in range(nseg):
                seg = slice(s * segw, (s + 1) * segw)
                B.op("vector", "tensor_scalar", kp_bf[c % 2][:, seg],
                     pair[s][:, 0:segw], 0.0, None, ALU.min, reads=pk[s],
                     writes=[spk])
                B.op("vector", "tensor_scalar",
                     kp_bf[c % 2][:, 1024 + s * segw:1024 + (s + 1) * segw],
                     pair[s][:, 0:segw], 0.0, None, ALU.max, reads=pk[s],
                     writes=[spk])
                B.op("scalar", "activation", kp_bf[c % 2][:, seg],
                     kp_bf[c % 2][:, seg], AF.Exp, reads=[spk], writes=[spk])
                B.op("vector", "tensor_tensor", af_sb[:, c, seg],
                     kp_bf[c % 2][:, seg],
                     kp_bf[c % 2][:, 1024 + s * segw:1024 + (s + 1) * segw],
                     ALU.add, reads=[spk], writes=[f"af.{c}"])
            yield
        if sw == 0 and finish_kv_hook[0] is not None:
            finish_kv_hook[0]()
        for dc in range(DC):
            for s in range(nseg):
                B.op("tensor", "matmul", pb[0 + s][0:HEADS, 0:segw],
                     centt[:, dc, :],
                     xnf[:, dc, XO + s * segw:XO + (s + 1) * segw],
                     start=(dc == 0), stop=(dc == DC - 1), acc=True,
                     reads=["centt"] + [f"xnf.{dc}.{t}.{P}" for t in seg_sts[s]],
                     writes=PBK[0 + s])
                B.op("tensor", "matmul", pb[2 + s][0:HEADS, 0:segw],
                     ks16[:, dc, :], af_sb[:, dc, s * segw:(s + 1) * segw],
                     start=(dc == 0), stop=(dc == DC - 1), acc=True,
                     reads=["ks16", f"af.{dc}"], writes=PBK[2 + s])
            if dc == 3:
                yield
        yield
        for s in range(nseg):
            seg = slice(s * segw, (s + 1) * segw)
            B.op("vector", "tensor_scalar", sc_sb[:, seg],
                 pb[0 + s][0:HEADS, 0:segw], eb_sb[:, :], None, ALU.add,
                 reads=PBK[0 + s] + ["eb"], writes=[f"sc.{s}"])
        rr = kp_scr[0][0:HEADS, 0:sw_tok]
        for s in range(nseg):
            seg = slice(s * segw, (s + 1) * segw)
            B.op("vector", "tensor_scalar", rr[:, seg],
                 pb[2 + s][0:HEADS, 0:segw], ATT_EPS, None, ALU.add,
                 reads=PBK[2 + s], writes=["kp0"])
        B.op("vector", "reciprocal", rr[:, :], rr[:, :], reads=["kp0"],
             writes=["kp0"])
        B.op("vector", "tensor_copy", rcb[:, :], rr[:, :], reads=["kp0"],
             writes=["rcb"])
        yield
        for c in range(DC):
            npair = (pb[0], pb[1]) if c % 2 == 0 else (pb[2], pb[3])
            npk = (PBK[0], PBK[1]) if c % 2 == 0 else (PBK[2], PBK[3])
            rpair = (pb[2], pb[3]) if c % 2 == 0 else (pb[0], pb[1])
            rpk = (PBK[2], PBK[3]) if c % 2 == 0 else (PBK[0], PBK[1])
            for s in range(nseg):
                B.op("tensor", "matmul", npair[s][:, 0:segw], kv_bd[:, c, :],
                     af_sb[:, c, s * segw:(s + 1) * segw], start=True,
                     stop=True, reads=["kv_bd", f"af.{c}"], writes=npk[s])
                B.op("tensor", "matmul", rpair[s][:, 0:segw], sel_sb[:, c, :],
                     rcb[:, s * segw:(s + 1) * segw], start=True, stop=True,
                     reads=["sel", "rcb"], writes=rpk[s])
                B.op("scalar", "copy", rb_sb[:, s, 0:segw],
                     rpair[s][:, 0:segw], reads=rpk[s], writes=[f"rb.{s}"])
                B.op("vector", "tensor_tensor",
                     af_sb[:, c, s * segw:(s + 1) * segw],
                     npair[s][:, 0:segw], rb_sb[:, s, 0:segw], ALU.mult,
                     reads=npk[s] + [f"rb.{s}"], writes=[f"af.{c}"])
            yield
        while True:
            yield

    def softmax_st(st, P):
        buf = st % 2
        sm = smx[buf]
        ss = smsc[buf]
        smk = f"sm{buf}."
        ssk = f"ss{buf}."
        scT = pb[4][:, buf * 16:buf * 16 + 16]
        B.op("tensor", "transpose", scT,
             sc_sb[:, st * 128:(st + 1) * 128], idn32[0:HEADS, 0:HEADS],
             reads=[f"sc.{st * 128 // segw}", "idn32"], writes=PBK[4])
        B.op("vector", "reduce_max", ss[:, 0:1], scT, AX.X, negate=True,
             reads=PBK[4], writes=[ssk + "n"])
        B.op("scalar", "activation", sm[:, 0, :], scT, AF.Exp,
             bias=ss[:, 0:1], accum_out=ss[:, 1:2],
             reads=PBK[4] + [ssk + "n"], writes=[smk + "0", ssk + "s"])
        B.op("vector", "reduce_max", ss[:, 2:3], sm[:, 0, :], AX.X,
             reads=[smk + "0"], writes=[ssk + "m"])
        B.op("vector", "tensor_scalar", sm[:, 1, :], sm[:, 0, :],
             ss[:, 2:3], None, ALU.is_equal,
             reads=[smk + "0", ssk + "m"], writes=[smk + "1"])
        B.op("vector", "tensor_tensor", sm[:, 2, :], sm[:, 1, :],
             sm[:, 0, :], ALU.mult, reads=[smk + "1", smk + "0"],
             writes=[smk + "2"])
        B.op("vector", "tensor_tensor", sm[:, 3, :], sm[:, 0, :],
             sm[:, 2, :], ALU.subtract, reads=[smk + "0", smk + "2"],
             writes=[smk + "3"])
        B.op("vector", "reduce_max", ss[:, 2:3], sm[:, 3, :], AX.X,
             reads=[smk + "3"], writes=[ssk + "m"])
        B.op("vector", "tensor_scalar", sm[:, 4, :], sm[:, 3, :],
             ss[:, 2:3], None, ALU.is_equal,
             reads=[smk + "3", ssk + "m"], writes=[smk + "4"])
        B.op("vector", "tensor_tensor", sm[:, 5, :], sm[:, 1, :],
             sm[:, 4, :], ALU.add, reads=[smk + "1", smk + "4"],
             writes=[smk + "5"])
        B.op("vector", "tensor_tensor", sm[:, 6, :], sm[:, 0, :],
             sm[:, 5, :], ALU.mult, reads=[smk + "0", smk + "5"],
             writes=[smk + "6"])
        B.op("vector", "reciprocal", ss[:, 1:2], ss[:, 1:2],
             reads=[ssk + "s"], writes=[ssk + "s"])
        B.op("vector", "tensor_scalar", g_tm[:, st, :], sm[:, 6, :],
             ss[:, 1:2], None, ALU.mult, reads=[smk + "6", ssk + "s"],
             writes=[f"g.{st}"])

    def main_sweep(sw, nxt):
        base = sw * sw_tok
        P = sw % 2
        XO = P * sw_tok
        if sw + 1 < sweeps:
            load_xnf(sw + 1)
        for pre_st in range(min(2, sw_sub)):
            B.dma("scalar", x_a[pre_st % 3][:, :],
                  x_in[base + pre_st * 128:base + (pre_st + 1) * 128, :],
                  writes=[f"x_a{pre_st % 3}"])
        softmax_st(0, P)
        for st in range(sw_sub):
            buf = st % 2
            gT = pb[5][0:HEADS, 0:128]
            B.op("tensor", "transpose", gT, g_tm[:, st, :], idn32[:, :],
                 reads=[f"g.{st}", "idn32"], writes=PBK[5])
            B.op("vector", "tensor_copy", g19[0:NE, :], gT, reads=PBK[5],
                 writes=["g19"])
            fpair = (pb[0], pb[1]) if buf == 0 else (pb[2], pb[3])
            fpk = (PBK[0], PBK[1]) if buf == 0 else (PBK[2], PBK[3])
            for half in range(2):
                B.op("tensor", "matmul", fpair[half][:, :], g19[:, :],
                     br19[:, half * 512:(half + 1) * 512], start=True,
                     stop=False, acc=True,
                     reads=["g19", "g19c", "br19", "br19x", "br19y"],
                     writes=fpk[half])
            for c in range(DC):
                B.op("tensor", "matmul",
                     fpair[c // 4][:, (c % 4) * 128:(c % 4) * 128 + 128],
                     xnf[:, c, XO + st * 128:XO + (st + 1) * 128], idnbf[:, :],
                     start=False, stop=False, acc=True,
                     reads=[f"xnf.{c}.{st}.{P}", "idnbf"], writes=fpk[c // 4])
            if st + 1 < sw_sub:
                softmax_st(st + 1, P)
            for c in range(DC):
                for half in range(2):
                    B.op("tensor", "matmul", fpair[half][:, :],
                         af_sb[:, c, st * 128:(st + 1) * 128],
                         wout_sb[:, c, half * 512:(half + 1) * 512],
                         start=False, stop=(c == DC - 1), acc=True,
                         reads=[f"af.{c}", "wout"], writes=fpk[half])
            xkey = f"x_a{st % 3}"
            if st + 2 < sw_sub:
                rows2 = slice(base + (st + 2) * 128, base + (st + 3) * 128)
                B.dma("scalar", x_a[(st + 2) % 3][:, :], x_in[rows2, :],
                      writes=[f"x_a{(st + 2) % 3}"])
            for half in range(2):
                B.op("vector", "tensor_tensor",
                     racc[:, st, half * 512:(half + 1) * 512],
                     fpair[half][:, :],
                     x_a[st % 3][:, half * 512:(half + 1) * 512], ALU.add,
                     reads=fpk[half] + [xkey], writes=[f"racc.{st}"])

        for e in range(NXP):
            wb = e % 2
            wsrc = wr_t if e < NE else ws_t
            eoff = (e if e < NE else e - NE) * D * D
            B.dma("gpsimd", wrt_bufs[wb],
                  bass.AP(wsrc, eoff, [[D, 128], [128 * D, DC], [1, D]]),
                  writes=wrt_buf_keys[wb])
            for st in range(sw_sub):
                par = (e * sw_sub + st) % 2
                xpair = (pb[4], pb[5]) if par == 0 else (pb[6], pb[7])
                xpk = (PBK[4], PBK[5]) if par == 0 else (PBK[6], PBK[7])
                for c in range(DC):
                    for half in range(2):
                        B.op("tensor", "matmul", xpair[half][:, :],
                             xnf[:, c, XO + st * 128:XO + (st + 1) * 128],
                             big_w[:, c, wb * D + half * 512:wb * D + (half + 1) * 512],
                             start=(c == 0), stop=(c == DC - 1), acc=True,
                             reads=[f"xnf.{c}.{st}.{P}"] + wrt_buf_keys[wb],
                             writes=xpk[half])
                gsc = g_tm[:, st, e:e + 1] if e < NE else 0.5
                grd = [f"g.{st}"] if e < NE else []
                for half in range(2):
                    B.op("vector", "scalar_tensor_tensor",
                         racc[:, st, half * 512:(half + 1) * 512],
                         xpair[half][:, :], gsc,
                         racc[:, st, half * 512:(half + 1) * 512],
                         ALU.mult, ALU.add,
                         reads=xpk[half] + grd + [f"racc.{st}"],
                         writes=[f"racc.{st}"])
            next(nxt)
            if e % 2 == 1:
                next(nxt)
        for _ in range(40):
            next(nxt)
        for st in range(sw_sub):
            rows = slice(base + st * 128, base + (st + 1) * 128)
            B.dma("scalar", y_out[rows, :], racc[:, st, :],
                  reads=[f"racc.{st}"])

    def empty_gen():
        while True:
            yield


    gens = [pre_sweep(sw) for sw in range(sweeps)]
    pa0_left = [2 if sweeps > 0 else 0]

    def a_xload(st):
        buf = st % 3
        B.dma("scalar", x_a[buf][:, :], x_in[st * 128:(st + 1) * 128, :],
              writes=[f"x_a{buf}"])

    assert a_subtiles % sw_sub == 0
    n_asw = a_subtiles // sw_sub

    ramp = min(sw_sub, a_subtiles)
    ln_seq = list(range(ramp))
    for _asw in range(a_subtiles // sw_sub):
        for _j in range(sw_sub):
            _st = _asw * sw_sub + _j
            if _asw == 0 and ramp < sw_sub and ramp + _j < sw_sub:
                ln_seq.append(ramp + _j)
            if _st + sw_sub < a_subtiles:
                ln_seq.append(_st + sw_sub)
    assert sorted(ln_seq) == list(range(a_subtiles))
    ln_pos = [0]

    def a_xload_buf(st, buf):
        B.dma("scalar", x_a[buf][:, :], x_in[st * 128:(st + 1) * 128, :],
              writes=[f"x_a{buf}"])

    def ln_store(s):
        pos = ln_pos[0]
        assert ln_seq[pos] == s, (pos, s, ln_seq[pos])
        if pos == 0:
            for ahead in range(min(2, len(ln_seq))):
                a_xload_buf(ln_seq[ahead], ahead % 3)
        if pos + 2 < len(ln_seq):
            a_xload_buf(ln_seq[pos + 2], (pos + 2) % 3)
        buf = pos % 3
        ln_subtile(buf, f"x_a{buf}")
        B.dma("gpsimd", xn_dram[s * 128:(s + 1) * 128, :],
              xn_tm[buf][:, :], reads=[f"xn_tm{buf}"], writes=[f"xnd.{s}"])
        ln_pos[0] += 1

    for st in range(ramp):
        ln_store(st)

    half_tok = ramp * 128
    for asw in range(n_asw):
        PA = (asw + 1) % 2
        XOA = PA * sw_tok
        if asw == 0 and ramp < sw_sub:
            load_xnf_rows(0, half_tok, PA, 0)
        else:
            load_xnf_at(asw * sw_tok, PA)
        for j in range(sw_sub):
            st = asw * sw_sub + j
            if asw == 0 and ramp < sw_sub and j == ramp:
                load_xnf_rows(half_tok, sw_tok - half_tok, PA, half_tok)
            buf2 = st % 2
            kpair = (pb[0], pb[1])
            vpair = (pb[2], pb[3])
            for c in range(DC):
                lhsT = xnf[:, c, XOA + j * 128:XOA + (j + 1) * 128]
                xk = f"xnf.{c}.{j}.{PA}"
                for half in range(2):
                    B.op("tensor", "matmul", kpair[half][:, :], lhsT,
                         big_w[:, c, half * 512:(half + 1) * 512],
                         start=(c == 0), stop=(c == DC - 1), acc=True,
                         reads=[xk, "bw0"], writes=PBK[half])
            ktk, vtk, spk = f"k_tm{buf2}", f"v_tm{buf2}", f"kp{buf2}"
            for half in range(2):
                seg = slice(half * 512, (half + 1) * 512)
                B.op("vector", "tensor_scalar", kp_bf[buf2][:, seg],
                     kpair[half][:, :], 0.0, None, ALU.min, reads=PBK[half],
                     writes=[spk])
                B.op("vector", "tensor_scalar",
                     kp_bf[buf2][:, 1024 + half * 512:1024 + (half + 1) * 512],
                     kpair[half][:, :], 0.0, None, ALU.max, reads=PBK[half],
                     writes=[spk])
            for c in range(DC):
                lhsT = xnf[:, c, XOA + j * 128:XOA + (j + 1) * 128]
                xk = f"xnf.{c}.{j}.{PA}"
                for half in range(2):
                    B.op("tensor", "matmul", vpair[half][:, :], lhsT,
                         big_w[:, c, D + half * 512:D + (half + 1) * 512],
                         start=(c == 0), stop=(c == DC - 1), acc=True,
                         reads=[xk, "bw1"], writes=PBK[2 + half])
            for half in range(2):
                seg = slice(half * 512, (half + 1) * 512)
                B.op("scalar", "activation", kp_bf[buf2][:, seg],
                     kp_bf[buf2][:, seg], AF.Exp, reads=[spk], writes=[spk])
                B.op("vector", "tensor_tensor", k_tm[buf2][:, seg],
                     kp_bf[buf2][:, seg],
                     kp_bf[buf2][:, 1024 + half * 512:1024 + (half + 1) * 512],
                     ALU.add, reads=[spk], writes=[ktk])
                B.op("scalar", "copy", v_tm[buf2][:, seg], vpair[half][:, :],
                     reads=PBK[2 + half], writes=[vtk])
            for p in range(DC):
                bk = 4 + p // 4
                B.op("tensor", "matmul",
                     pb[bk][:, 128 * (p % 4):128 * (p % 4) + 128],
                     k_tm[buf2][:, 128 * p:128 * p + 128],
                     v_tm[buf2][:, 128 * p:128 * p + 128],
                     start=(st == 0 and p % 4 == 0),
                     stop=(st == a_subtiles - 1 and p % 4 == 3),
                     acc=True, reads=[ktk, vtk], writes=PBK[bk])
                B.op("tensor", "matmul", pb[7][:, 256 + p:256 + p + 1],
                     k_tm[buf2][:, 128 * p:128 * p + 128], ones_col[:, :],
                     start=(p == 0), stop=(p == DC - 1), acc=True,
                     reads=[ktk, "ones"], writes=["pb7.k"])
            if asw == 0 and ramp < sw_sub and ramp + j < sw_sub:
                ln_store(ramp + j)
            if st + sw_sub < a_subtiles:
                ln_store(st + sw_sub)
            B.op("vector", "tensor_tensor", ksum_acc[:, :], ksum_acc[:, :],
                 pb[7][:, 256:256 + DC], ALU.add, reads=["pb7.k", "ksacc"],
                 writes=["ksacc"])
            if (sweeps > 0 and st >= a_subtiles - sw_sub + 1
                    and pa0_left[0] > 0):
                next(gens[0])
                pa0_left[0] -= 1

    if cc_groups is not None:
        B.op("vector", "tensor_copy", cc_sb[:, 0:512], pb[4][:, :],
             reads=PBK[4], writes=["ccs"])
        B.op("vector", "tensor_copy", cc_sb[:, 512:1024], pb[5][:, :],
             reads=PBK[5], writes=["ccs"])
        B.op("vector", "tensor_copy", cc_sb[:, 1024:1032], ksum_acc[:, :],
             reads=["ksacc"], writes=["ccs"])
        B.dma("gpsimd", cc_in[:, :], cc_sb[:, :], reads=["ccs"],
              writes=["ccin"])
        B.op("gpsimd", "collective_compute", "AllReduce", ALU.add,
             replica_groups=cc_groups, ins=[cc_in.ap().opt()],
             outs=[cc_out.ap().opt()], reads=["ccin"], writes=["ccout"],
             queue="qcc", force_inc=True)
        B.dma("gpsimd", cc_sb[:, :], cc_out[:, :], reads=["ccout"],
              writes=["ccs"])

        def finish_kv():
            for p in range(DC):
                blk = cc_sb[:, 128 * p:128 * p + 128]
                B.op("vector", "tensor_copy", kv_bd[0:64, p, 0:64],
                     blk[0:64, 0:64], reads=["ccs"], writes=["kv_bd"])
                B.op("vector", "tensor_copy", kv_bd[64:128, p, 64:128],
                     blk[64:128, 64:128], reads=["ccs"], writes=["kv_bd"])
                B.op("vector", "tensor_copy", ks16[0:64, p, 2 * p:2 * p + 1],
                     cc_sb[0:64, 1024 + p:1025 + p], reads=["ccs"],
                     writes=["ks16"])
                B.op("vector", "tensor_copy",
                     ks16[64:128, p, 2 * p + 1:2 * p + 2],
                     cc_sb[64:128, 1024 + p:1025 + p], reads=["ccs"],
                     writes=["ks16"])
        finish_kv_hook[0] = finish_kv
    else:
        for p in range(DC):
            bk = 4 + p // 4
            blk = pb[bk][:, 128 * (p % 4):128 * (p % 4) + 128]
            B.op("vector", "tensor_copy", kv_bd[0:64, p, 0:64],
                 blk[0:64, 0:64], reads=PBK[bk], writes=["kv_bd"])
            B.op("vector", "tensor_copy", kv_bd[64:128, p, 64:128],
                 blk[64:128, 64:128], reads=PBK[bk], writes=["kv_bd"])
            B.op("vector", "tensor_copy", ks16[0:64, p, 2 * p:2 * p + 1],
                 ksum_acc[0:64, p:p + 1], reads=["ksacc"], writes=["ks16"])
            B.op("vector", "tensor_copy", ks16[64:128, p, 2 * p + 1:2 * p + 2],
                 ksum_acc[64:128, p:p + 1], reads=["ksacc"], writes=["ks16"])

    for _ in range(100):
        next(gens[0])
    for sw in range(sweeps):
        nxt = gens[sw + 1] if sw + 1 < sweeps else empty_gen()
        main_sweep(sw, nxt)

    B.finalize()
    return nc


_NC_CACHE = {}


CC_GROUPS = [[0, 1], [2, 3], [4, 5], [6, 7]]


def _get_nc():
    if "full" not in _NC_CACHE:
        _NC_CACHE["full"] = build(a_subtiles=32, cc_groups=CC_GROUPS)
    return _NC_CACHE["full"]


def kernel(L0_attn_x, L0_attn_Wqkv, L0_attn_Wout, L0_attn_bout,
           L0_ffn_centroids, L0_ffn_expert_biases, L0_ffn_Wr, L0_ffn_br,
           L0_ffn_Ws, L0_ffn_bs):
    x = np.ascontiguousarray(np.asarray(L0_attn_x, dtype=np.float32))
    Bsz, N, _ = x.shape
    half = N // 2
    nc = _get_nc()
    common = {
        "wqkv": np.ascontiguousarray(L0_attn_Wqkv, dtype=np.float32),
        "wout": np.ascontiguousarray(L0_attn_Wout, dtype=np.float32),
        "bout": np.ascontiguousarray(L0_attn_bout, dtype=np.float32).reshape(1, D),
        "cent": np.ascontiguousarray(L0_ffn_centroids, dtype=np.float32),
        "ebias": np.ascontiguousarray(L0_ffn_expert_biases,
                                      dtype=np.float32).reshape(HEADS, 1),
        "wr_t": np.ascontiguousarray(
            np.asarray(L0_ffn_Wr, dtype=np.float32).transpose(0, 2, 1)),
        "brin": np.ascontiguousarray(L0_ffn_br, dtype=np.float32),
        "ws_t": np.ascontiguousarray(
            np.asarray(L0_ffn_Ws, dtype=np.float32).transpose(0, 2, 1)),
        "bsin": np.ascontiguousarray(L0_ffn_bs, dtype=np.float32),
    }
    in_maps = []
    for i in range(8):
        b, h = i // 2, i % 2
        m = dict(common)
        m["x"] = np.ascontiguousarray(x[b, h * half:(h + 1) * half, :])
        in_maps.append(m)
    res = run_bass_kernel_spmd(nc, in_maps, core_ids=list(range(8)))
    y = np.empty((Bsz, N, D), dtype=np.float32)
    for i in range(8):
        b, h = i // 2, i % 2
        y[b, h * half:(h + 1) * half, :] = res.results[i]["y"]
    return y
